# version 03
# baseline (speedup 1.0000x reference)
import numpy as np
import concourse.bass as bass
import concourse.mybir as mybir
from concourse.bass_utils import run_bass_kernel_spmd

F32 = mybir.dt.float32
BF16 = mybir.dt.bfloat16
AF = mybir.ActivationFunctionType
ALU = mybir.AluOpType

D = 2048
ND = 16
T = 2048
NCH = 4
CH = 512
NQ = 16
S = 4096
NH = 16
NG = 4
HD = 128
NREXP = 7

S_HI = float(np.float32(1.0 / np.sqrt(np.float64(128.0))))
S_LO = float(np.float32(np.float64(1.0 / np.sqrt(np.float64(128.0))) - np.float64(np.float32(1.0 / np.sqrt(np.float64(128.0))))))

KPART = 512 * T
VPART = T * 512


class Sem:
    __slots__ = ("h", "n")

    def __init__(self, nc, name):
        self.h = nc.semaphore(name).__enter__()
        self.n = 0


class Prog:
    ENGINES = ("tensor", "vector", "scalar", "sync", "gpsimd")

    def __init__(self, nc):
        self.nc = nc
        self.qs = {e: [] for e in self.ENGINES}
        self.lastw = {e: {} for e in self.ENGINES}

    def op(self, eng, build, waits=(), incs=()):
        wl = []
        for s, v in waits:
            if v > 0 and self.lastw[eng].get(id(s), -1) < v:
                self.lastw[eng][id(s)] = v
                wl.append((s.h, v))
        rec = []
        for s, k in incs:
            s.n += k
            rec.append(s.n)

        def f(e, build=build, wl=tuple(wl), incs=tuple(incs)):
            for h, v in wl:
                e.wait_ge(h, v)
            inst = build(e)
            if isinstance(inst, (list, tuple)):
                inst = inst[-1]
            for (s, k), _ in zip(incs, incs):
                inst.then_inc(s.h, k)

        self.qs[eng].append(f)
        return rec

    def finish(self, block):
        for name in self.ENGINES:
            fns = self.qs[name]
            if not fns:
                continue

            def section(e, fns=fns):
                for f in fns:
                    f(e)

            getattr(block, name)(section)


def w(sem):
    return (sem, sem.n)


class Ring:

    def __init__(self, P, name, bufs):
        self.P = P
        self.bufs = bufs
        self.nsl = len(bufs)
        self.s_ld = Sem(P.nc, name + "_ld")
        self.free = [None] * len(bufs)
        self.li = 0

    def load(self, src_ap, extra_waits=()):
        slot = self.li % self.nsl
        waits = list(extra_waits)
        if self.free[slot] is not None:
            waits.append(self.free[slot])
        self.li += 1
        buf = self.bufs[slot]
        self.P.op("gpsimd", lambda e, b=buf, s=src_ap: e.dma_start(out=b[:], in_=s),
                  waits=waits, incs=[(self.s_ld, 16)])
        return buf, self.s_ld.n, slot


def build_nc(n_sub=4):
    nc = bass.Bass("TRN2", target_bir_lowering=False, debug=False)
    P = Prog(nc)

    din = {}

    def dri(name, shape):
        din[name] = nc.dram_tensor(name, shape, F32, kind="ExternalInput").ap()

    dri("xT", [D, T])
    dri("aw_q", [D, D]); dri("aw_kv", [D, 1024]); dri("aw_d", [D, D])
    dri("bw_q", [D, D]); dri("bw_k", [D, 512]); dri("bw_v", [D, 512]); dri("bw_o", [D, D])
    dri("f_w1", [NREXP, D, D]); dri("f_wg", [NREXP, D, D]); dri("f_w2", [NREXP, D, D])
    dri("f_s1", [D, D]); dri("f_sg", [D, D]); dri("f_s2", [D, D])
    dri("f_gw", [D, NREXP]); dri("f_gb", [NREXP, 1]); dri("f_bias", [128, NREXP])
    dri("masks", [8, 128, CH])
    dri("rc_hi", [128, T]); dri("rc_lo", [128, T])
    dri("rs_hi", [128, T]); dri("rs_lo", [128, T])
    dri("ident", [128, 128])
    dri("sel7", [NREXP, NREXP * 128])
    xout = nc.dram_tensor("xout", [D, T], F32, kind="ExternalOutput").ap()
    x_hbm = nc.dram_tensor("x_hbm", [ND, 128, T], F32, kind="Internal").ap()
    cc_in = nc.dram_tensor("cc_in", [KPART + VPART], F32, kind="Internal").ap()
    cc_out = nc.dram_tensor("cc_out", [2, KPART + VPART], F32, kind="Internal").ap()

    ARENA = 207424
    arena = nc.alloc_sbuf_tensor("arena", [128, ARENA + 256], mybir.dt.uint8)
    B0 = (nc.lookup_mloc(arena).addr + 255) & ~255

    def A(name, shape, dt, *, offset):
        return nc.alloc_sbuf_tensor_at(name, shape, dt, offset=B0 + offset)

    reg1 = A("reg1", [128, 32768], BF16, offset=0)
    reg2 = A("reg2", [128, 32768], BF16, offset=65536)
    rAb = [A(f"rA{i}", [128, ND, 128], BF16, offset=131072 + 4096 * i) for i in range(2)]
    rBb = [A(f"rB{i}", [128, ND, 128], BF16, offset=139264 + 4096 * i) for i in range(2)]
    rCb = [A(f"rC{i}", [128, ND, 128], BF16, offset=147456 + 4096 * i) for i in range(2)]
    ctx_sb = A("ctx_sb", [128, NH, CH], BF16, offset=139264)
    xq = A("xq", [128, ND, 128], F32, offset=147456)
    tabs_c = A("tabs_c", [128, T], F32, offset=155648)
    tabs_s = A("tabs_s", [128, T], F32, offset=163840)
    bmask = A("bmask", [128, 8, CH], BF16, offset=155648)
    kstage = [A(f"kst{i}", [128, CH], F32, offset=172032 + 2048 * i) for i in range(2)]
    tq = [A(f"tq{i}", [128, CH], F32, offset=176128 + 2048 * i) for i in range(2)]
    vstage = A("vstage", [128, 128], F32, offset=180224)
    p_sb = A("p_sb", [128, 4, CH], BF16, offset=180736)
    rr_sb = A("rr_sb", [1, CH], F32, offset=184832)
    bc_sb = A("bc_sb", [128, CH], F32, offset=186880)
    wbc_sb = A("wbc_sb", [128, NCH, CH], BF16, offset=172032)
    ssb = [A(f"ssb{i}", [128, CH], BF16, offset=176128 + 1024 * i) for i in range(2)]
    wT_sb = A("wT_sb", [NREXP, T], BF16, offset=178176)
    xacc = [A(f"xacc{i}", [128, CH], F32, offset=188928 + 2048 * i) for i in range(2)]
    xb_sb = A("xb_sb", [128, ND, 128], BF16, offset=193024)
    sq_sb = A("sq_sb", [128, ND, 128], BF16, offset=197120)
    gl_sb = A("gl_sb", [NREXP, CH], F32, offset=197120)
    gt_sb = A("gt_sb", [128, ND, NREXP], F32, offset=199168)
    b0_sb = A("b0_sb", [128, ND, NREXP], F32, offset=199616)
    bm_sb = A("bm_sb", [128, ND, NREXP], F32, offset=200064)
    t7_sb = A("t7_sb", [128, ND, NREXP], F32, offset=200512)
    mx_sb = A("mx_sb", [128, ND, 1], F32, offset=200960)

    off = [201216]

    def sm(name, shape, dt):
        t = A(name, shape, dt, offset=off[0])
        nb = int(np.prod(shape[1:])) * (2 if dt == BF16 else 4)
        off[0] += (nb + 31) & ~31
        return t

    ones_bf = sm("ones_bf", [128, 1], BF16)
    ones_f1 = sm("ones_f1", [1, 128], F32)
    ones_b1 = sm("ones_b1", [1, 128], BF16)
    m_sb = sm("m_sb", [1, 128], F32)
    v_sb = sm("v_sb", [1, 128], F32)
    r_sb = sm("r_sb", [1, 128], F32)
    mr_sb = sm("mr_sb", [1, 128], F32)
    t1_sb = sm("t1_sb", [1, 128], F32)
    gw_sb = sm("gw_sb", [128, ND, NREXP], BF16)
    gb_sb = sm("gb_sb", [NREXP, 1], F32)
    bias_sb = sm("bias_sb", [128, NREXP], F32)
    id_sb = sm("id_sb", [128, 128], F32)
    sel_sb = sm("sel_sb", [NREXP, NREXP * 128], BF16)
    assert off[0] <= ARENA, off[0]

    ps = [nc.alloc_psum_tensor(f"ps{i}", [128, CH], F32) for i in range(8)]
    pg = {i: None for i in range(8)}

    def bank_wait(i):
        return [pg[i]] if pg[i] else []

    sPE = Sem(nc, "sPE")
    sDV = Sem(nc, "sDV")
    sAC = Sem(nc, "sAC")
    sXL = Sem(nc, "sXL")
    sXS = Sem(nc, "sXS")
    sCI = Sem(nc, "sCI")
    sCC = Sem(nc, "sCC")
    sKV = Sem(nc, "sKV")
    sMI = Sem(nc, "sMI")
    sOUT = Sem(nc, "sOUT")
    rA = Ring(P, "rA", rAb)
    rB = Ring(P, "rB", rBb)
    rC = Ring(P, "rC", rCb)

    epoch = {}
    state = {
        "xq_free": (sDV, 0),
        "xacc_free": [(sDV, 0), (sDV, 0)],
        "kst_free": [(sCI, 0), (sCI, 0)],
        "ssb_free": [(sDV, 0), (sDV, 0)],
        "vst_free": (sCI, 0),
    }

    xnv = reg1[:].rearrange("p (n t) -> p n t", n=ND)
    Kv = reg1[:, 0:16384].rearrange("p (g k) -> p g k", g=NG)
    Vv = reg1[:, 16384:32768].rearrange("p (k e) -> p k e", k=32)
    qv = reg2[:].rearrange("p (n t) -> p n t", n=NH)
    uv = reg2[:].rearrange("p (n t) -> p n t", n=ND)

    P.op("vector", lambda e: e.memset(ones_bf[:], 1.0), incs=[(sDV, 1)])
    P.op("vector", lambda e: e.memset(ones_f1[:], 1.0), incs=[(sDV, 1)])
    P.op("vector", lambda e: e.memset(ones_b1[:], 1.0), incs=[(sDV, 1)])
    consts_dv = sDV.n
    P.op("sync", lambda e: e.dma_start(out=id_sb[:], in_=din["ident"]), incs=[(sMI, 16)])
    P.op("sync", lambda e: e.dma_start(out=gb_sb[:], in_=din["f_gb"]), incs=[(sMI, 16)])
    P.op("sync", lambda e: e.dma_start(out=bias_sb[:], in_=din["f_bias"]), incs=[(sMI, 16)])
    P.op("gpsimd", lambda e: e.dma_start(
        out=gw_sb[:], in_=din["f_gw"].rearrange("(n p) k -> p n k", p=128)),
        incs=[(sMI, 16)])
    P.op("gpsimd", lambda e: e.dma_start(out=sel_sb[:], in_=din["sel7"]),
        incs=[(sMI, 16)])
    consts_mi = sMI.n

    def emit_ln(sub, xsrc):
        ln_done = [0] * NCH
        for qc in range(NQ):
            c = qc // 4
            ts = slice(qc * 128, qc * 128 + 128)
            waits = [state["xq_free"]]
            ep = max((epoch.get((dt, c), 0) for dt in range(ND)), default=0)
            if ep:
                waits.append((sXS, ep))
            P.op("sync", lambda e, s=xsrc, ts=ts: e.dma_start(
                out=xq[:], in_=s[:, :, ts].rearrange("n p t -> p n t")),
                 waits=waits, incs=[(sXL, 16)])
            P.op("vector", lambda e: e.tensor_copy(xb_sb[:], xq[:]),
                 waits=[w(sXL)], incs=[(sDV, 1)])
            P.op("scalar", lambda e: e.activation(sq_sb[:], xb_sb[:], AF.Square),
                 waits=[w(sDV)], incs=[(sAC, 1)])

            def statmm(e):
                for dt in range(ND):
                    e.matmul(ps[7][0:1, 0:128], ones_bf[:], xb_sb[:, dt, :],
                             start=(dt == 0), stop=(dt == ND - 1))
                for dt in range(ND):
                    i = e.matmul(ps[7][0:1, 128:256], ones_bf[:], sq_sb[:, dt, :],
                                 start=False, stop=(dt == ND - 1))
                return i
            P.op("tensor", statmm, waits=[w(sDV), w(sAC)] + bank_wait(7), incs=[(sPE, 1)])

            def stats(e):
                e.tensor_scalar_mul(m_sb[:], ps[7][0:1, 0:128], 1.0 / 2048.0)
                e.tensor_scalar_mul(v_sb[:], ps[7][0:1, 128:256], 1.0 / 2048.0)
                e.tensor_mul(t1_sb[:], m_sb[:], m_sb[:])
                e.tensor_sub(v_sb[:], v_sb[:], t1_sb[:])
                return e.tensor_scalar_add(v_sb[:], v_sb[:], 1e-5)
            rec = P.op("vector", stats, waits=[w(sPE)], incs=[(sDV, 1)])
            pg[7] = (sDV, rec[0])
            P.op("scalar", lambda e: e.activation(r_sb[:], v_sb[:], AF.Sqrt),
                 waits=[w(sDV)], incs=[(sAC, 1)])
            P.op("vector", lambda e: e.reciprocal(r_sb[:], r_sb[:]),
                 waits=[w(sAC)], incs=[(sDV, 1)])
            P.op("vector", lambda e: e.tensor_mul(mr_sb[:], m_sb[:], r_sb[:]),
                 waits=[w(sDV)], incs=[(sDV, 1)])

            def bcast(e):
                e.matmul(ps[6][:, 0:128], ones_f1[:], r_sb[:], start=True, stop=True)
                return e.matmul(ps[6][:, 128:256], ones_f1[:], mr_sb[:], start=False, stop=True)
            P.op("tensor", bcast, waits=[w(sDV)] + bank_wait(6), incs=[(sPE, 1)])

            def norm(e, ts=ts):
                rb = ps[6][:, 0:128].unsqueeze(1).broadcast_to([128, ND, 128])
                mb = ps[6][:, 128:256].unsqueeze(1).broadcast_to([128, ND, 128])
                e.tensor_mul(xq[:], xq[:], rb)
                return e.tensor_sub(xnv[:, :, ts], xq[:], mb)
            rec = P.op("vector", norm, waits=[w(sPE)], incs=[(sDV, 1)])
            pg[6] = (sDV, rec[0])
            state["xq_free"] = (sDV, rec[0])
            if qc % 4 == 3:
                ln_done[c] = sDV.n
        return ln_done

    def emit_gate(ln_done):
        for c in range(NCH):
            def glog(e, c=c):
                for dt in range(ND):
                    i = e.matmul(ps[5][0:NREXP, :], gw_sb[:, dt, :],
                                 xnv[:, dt, c * CH:(c + 1) * CH],
                                 start=(dt == 0), stop=(dt == ND - 1))
                return i
            P.op("tensor", glog, waits=[(sMI, consts_mi), (sDV, ln_done[c])] + bank_wait(5),
                 incs=[(sPE, 1)])
            rec = P.op("vector", lambda e: e.tensor_scalar_add(gl_sb[:], ps[5][0:NREXP, :],
                                                               gb_sb[:, 0:1]),
                       waits=[w(sPE)], incs=[(sDV, 1)])
            pg[5] = (sDV, rec[0])

            def gtr(e):
                for part in range(4):
                    i = e.transpose(ps[4][:, part * NREXP:(part + 1) * NREXP],
                                    gl_sb[0:NREXP, part * 128:(part + 1) * 128],
                                    id_sb[0:NREXP, 0:NREXP])
                return i
            P.op("tensor", gtr, waits=[w(sDV), (sMI, consts_mi)] + bank_wait(4),
                 incs=[(sPE, 1)])
            rec = P.op("scalar", lambda e, c=c: e.activation(
                gt_sb[:, 4 * c:4 * c + 4, :].rearrange("p a b -> p (a b)"),
                ps[4][:, 0:4 * NREXP], AF.Copy),
                waits=[w(sPE)], incs=[(sAC, 1)])
            pg[4] = (sAC, rec[0])

        def bc7T(t):
            return t[:].rearrange("p a b -> p b a").broadcast_to([128, NREXP, ND])

        def trv(t):
            return t[:].rearrange("p a b -> p b a")

        P.op("vector", lambda e: e.tensor_reduce(mx_sb[:], gt_sb[:], mybir.AxisListType.X, ALU.max),
             waits=[w(sAC)], incs=[(sDV, 1)])
        P.op("vector", lambda e: e.tensor_sub(trv(t7_sb), trv(gt_sb), bc7T(mx_sb)),
             waits=[w(sDV)], incs=[(sDV, 1)])
        P.op("scalar", lambda e: e.activation(t7_sb[:], t7_sb[:], AF.Exp),
             waits=[w(sDV)], incs=[(sAC, 1)])
        P.op("vector", lambda e: e.tensor_reduce(mx_sb[:], t7_sb[:], mybir.AxisListType.X, ALU.add),
             waits=[w(sAC)], incs=[(sDV, 1)])
        P.op("vector", lambda e: e.reciprocal(mx_sb[:], mx_sb[:]),
             waits=[w(sDV)], incs=[(sDV, 1)])
        P.op("vector", lambda e: e.tensor_mul(trv(gt_sb), trv(t7_sb), bc7T(mx_sb)),
             waits=[w(sDV)], incs=[(sDV, 1)])
        P.op("vector", lambda e: e.tensor_add(
            b0_sb[:], gt_sb[:], bias_sb[:].unsqueeze(1).broadcast_to([128, ND, NREXP])),
            waits=[(sMI, consts_mi)], incs=[(sDV, 1)])
        P.op("vector", lambda e: e.tensor_copy(bm_sb[:], b0_sb[:]), incs=[(sDV, 1)])
        for _ in range(2):
            P.op("vector", lambda e: e.tensor_reduce(mx_sb[:], bm_sb[:], mybir.AxisListType.X, ALU.max),
                 incs=[(sDV, 1)])
            P.op("vector", lambda e: e.tensor_tensor(trv(t7_sb), trv(bm_sb), bc7T(mx_sb), ALU.is_equal),
                 waits=[w(sDV)], incs=[(sDV, 1)])
            P.op("vector", lambda e: e.scalar_tensor_tensor(bm_sb[:], t7_sb[:], -100.0, bm_sb[:],
                                                            ALU.mult, ALU.add),
                 incs=[(sDV, 1)])
        P.op("vector", lambda e: e.tensor_reduce(mx_sb[:], bm_sb[:], mybir.AxisListType.X, ALU.max),
             incs=[(sDV, 1)])
        P.op("vector", lambda e: e.tensor_tensor(trv(t7_sb), trv(b0_sb), bc7T(mx_sb), ALU.is_ge),
             waits=[w(sDV)], incs=[(sDV, 1)])
        P.op("vector", lambda e: e.tensor_mul(gt_sb[:], gt_sb[:], t7_sb[:]), incs=[(sDV, 1)])
        P.op("vector", lambda e: e.tensor_reduce(mx_sb[:], gt_sb[:], mybir.AxisListType.X, ALU.add),
             incs=[(sDV, 1)])
        P.op("vector", lambda e: e.reciprocal(mx_sb[:], mx_sb[:]),
             waits=[w(sDV)], incs=[(sDV, 1)])
        P.op("vector", lambda e: e.tensor_mul(trv(gt_sb), trv(gt_sb), bc7T(mx_sb)),
             waits=[w(sDV)], incs=[(sDV, 1)])
        wdv = sDV.n
        for grp in range(4):
            def wtr4(e, grp=grp):
                for j in range(4):
                    tt = grp * 4 + j
                    i = e.transpose(ps[4][0:NREXP, j * 128:(j + 1) * 128],
                                    gt_sb[:, tt, :], id_sb[:])
                return i
            P.op("tensor", wtr4, waits=[(sDV, wdv)] + bank_wait(4), incs=[(sPE, 1)])
            rec = P.op("scalar", lambda e, grp=grp: e.activation(
                wT_sb[0:NREXP, grp * CH:(grp + 1) * CH], ps[4][0:NREXP, 0:CH], AF.Copy),
                waits=[w(sPE)], incs=[(sAC, 1)])
            pg[4] = (sAC, rec[0])
        return sAC.n

    def emit_ffn(sub, ln_done, xsrc_first, is_last):
        gate_tick = emit_gate(ln_done)
        xov = xout.rearrange("(n p) t -> n p t", p=128)
        for ex in range(1 + NREXP):
            routed = ex >= 1
            if routed:
                w1s, wgs, w2s = din["f_w1"][ex - 1], din["f_wg"][ex - 1], din["f_w2"][ex - 1]
            else:
                w1s, wgs, w2s = din["f_s1"], din["f_sg"], din["f_s2"]
            xsrc = xsrc_first if ex == 0 else x_hbm
            if routed:
                for c in range(NCH):
                    P.op("tensor", lambda e, re=ex - 1, c=c: e.matmul(
                        ps[6][:], sel_sb[:, re * 128:(re + 1) * 128],
                        wT_sb[:, c * CH:(c + 1) * CH],
                        start=True, stop=True),
                        waits=[(sAC, gate_tick), (sMI, consts_mi)] + bank_wait(6),
                        incs=[(sPE, 1)])
                    rec = P.op("vector", lambda e, c=c: e.tensor_copy(wbc_sb[:, c, :], ps[6][:]),
                               waits=[w(sPE)], incs=[(sDV, 1)])
                    pg[6] = (sDV, rec[0])
            for ht in range(ND):
                sa, t1, slA = rA.load(w1s.rearrange("(n p) m -> p n m", p=128)[:, :, ht * 128:(ht + 1) * 128])
                sg, t2, slB = rB.load(wgs.rearrange("(n p) m -> p n m", p=128)[:, :, ht * 128:(ht + 1) * 128])
                for c in range(NCH):
                    ia, ig = 0 + (c % 2), 2 + (c % 2)

                    def ag(e, sa=sa, sg=sg, c=c, ia=ia, ig=ig):
                        for dt in range(ND):
                            e.matmul(ps[ia][:], sa[:, dt, :], xnv[:, dt, c * CH:(c + 1) * CH],
                                     start=(dt == 0), stop=(dt == ND - 1))
                        for dt in range(ND):
                            i = e.matmul(ps[ig][:], sg[:, dt, :], xnv[:, dt, c * CH:(c + 1) * CH],
                                         start=(dt == 0), stop=(dt == ND - 1))
                        return i
                    rec = P.op("tensor", ag,
                               waits=[(rA.s_ld, t1), (rB.s_ld, t2), (sDV, ln_done[c])]
                               + bank_wait(ia) + bank_wait(ig), incs=[(sPE, 1)])
                    if c == NCH - 1:
                        rA.free[slA] = (sPE, rec[0])
                        rB.free[slB] = (sPE, rec[0])
                    sl = (ht * NCH + c) % 2
                    rec = P.op("scalar", lambda e, ia=ia, sl=sl: e.activation(
                        ssb[sl][:], ps[ia][:], AF.Silu),
                        waits=[w(sPE), state["ssb_free"][sl]], incs=[(sAC, 1)])
                    pg[ia] = (sAC, rec[0])

                    def umul(e, sl=sl, ig=ig, ht=ht, c=c, routed=routed):
                        usl = uv[:, ht, c * CH:(c + 1) * CH]
                        i = e.tensor_mul(usl, ssb[sl][:], ps[ig][:])
                        if routed:
                            i = e.tensor_mul(usl, usl, wbc_sb[:, c, :])
                        return i
                    rec = P.op("vector", umul, waits=[w(sAC)], incs=[(sDV, 1)])
                    pg[ig] = (sDV, rec[0])
                    state["ssb_free"][sl] = (sDV, rec[0])
            u_ready = sDV.n
            for dt in range(ND):
                ew = [(sDV, ln_done[NCH - 1])] if (ex == 0 and dt < 2) else ()
                s2, t3, slC = rC.load(w2s.rearrange("(n p) m -> p n m", p=128)[:, :, dt * 128:(dt + 1) * 128],
                                      extra_waits=ew)
                for c in range(NCH):
                    iy = 4 + (c % 2)

                    def w2mm(e, s2=s2, c=c, iy=iy):
                        for ht in range(ND):
                            i = e.matmul(ps[iy][:], s2[:, ht, :], uv[:, ht, c * CH:(c + 1) * CH],
                                         start=(ht == 0), stop=(ht == ND - 1))
                        return i
                    rec = P.op("tensor", w2mm, waits=[(rC.s_ld, t3), (sDV, u_ready)] + bank_wait(iy),
                               incs=[(sPE, 1)])
                    if c == NCH - 1:
                        rC.free[slC] = (sPE, rec[0])
                    asl = (dt * NCH + c) % 2
                    lw = [state["xacc_free"][asl]]
                    ep = epoch.get((dt, c), 0)
                    if ep:
                        lw.append((sXS, ep))
                    P.op("sync", lambda e, s=xsrc, dt=dt, c=c, a=asl: e.dma_start(
                        out=xacc[a][:], in_=s[dt, :, c * CH:(c + 1) * CH]),
                        waits=lw, incs=[(sXL, 16)])
                    rec = P.op("vector", lambda e, a=asl, iy=iy: e.tensor_add(
                        xacc[a][:], ps[iy][:], xacc[a][:]),
                        waits=[w(sPE), w(sXL)], incs=[(sDV, 1)])
                    pg[iy] = (sDV, rec[0])
                    state["xacc_free"][asl] = (sDV, rec[0])
                    last = is_last and ex == NREXP
                    dst = xov if last else x_hbm
                    ssem = sOUT if last else sXS
                    P.op("sync", lambda e, d=dst, dt=dt, c=c, a=asl: e.dma_start(
                        out=d[dt, :, c * CH:(c + 1) * CH], in_=xacc[a][:]),
                        waits=[w(sDV)], incs=[(ssem, 16)])
                    if not last:
                        epoch[(dt, c)] = sXS.n

    def emit_attn(sub, kind, ln_done, xsrc):
        is_a = kind == "a"
        if is_a:
            wq_src, wo_src = din["aw_q"], din["aw_d"]
            wk_src, wv_src = din["aw_kv"][:, 0:512], din["aw_kv"][:, 512:1024]
        else:
            wq_src, wo_src = din["bw_q"], din["bw_o"]
            wk_src, wv_src = din["bw_k"], din["bw_v"]
        tab_tick = 0
        mask_tick = 0
        if is_a:
            P.op("gpsimd", lambda e: e.dma_start(
                out=bmask[:], in_=din["masks"].rearrange("g p m -> p g m")),
                incs=[(sMI, 16)])
            mask_tick = sMI.n
        else:
            P.op("sync", lambda e: e.dma_start(out=tabs_c[:], in_=din["rc_hi"]),
                 incs=[(sMI, 16)])
            P.op("sync", lambda e: e.dma_start(out=tabs_s[:], in_=din["rs_hi"]),
                 incs=[(sMI, 16)])
            for j, (nm, dst) in enumerate((("rc_lo", tabs_c), ("rs_lo", tabs_s))):
                for c4 in range(4):
                    sl = c4 % 2
                    P.op("sync", lambda e, nm=nm, c4=c4, sl=sl: e.dma_start(
                        out=tq[sl][:], in_=din[nm][:, c4 * CH:(c4 + 1) * CH]),
                        waits=[w(sDV)], incs=[(sMI, 16)])
                    P.op("vector", lambda e, dst=dst, c4=c4, sl=sl: e.tensor_add(
                        dst[:, c4 * CH:(c4 + 1) * CH], dst[:, c4 * CH:(c4 + 1) * CH],
                        tq[sl][:]),
                        waits=[w(sMI)], incs=[(sDV, 1)])
            tab_tick = sDV.n
        for et in range(NH):
            sa, t1, slA = rA.load(wq_src.rearrange("(n p) m -> p n m", p=128)[:, :, et * 128:(et + 1) * 128])
            for c in range(NCH):
                iq = 0 + (c % 2)

                def qmm(e, sa=sa, c=c, iq=iq):
                    for dt in range(ND):
                        i = e.matmul(ps[iq][:], sa[:, dt, :], xnv[:, dt, c * CH:(c + 1) * CH],
                                     start=(dt == 0), stop=(dt == ND - 1))
                    return i
                rec0 = P.op("tensor", qmm, waits=[(rA.s_ld, t1), (sDV, ln_done[c])] + bank_wait(iq),
                            incs=[(sPE, 1)])
                if c == NCH - 1:
                    rA.free[slA] = (sPE, rec0[0])
                if is_a:
                    rec = P.op("vector", lambda e, et=et, c=c, iq=iq: e.tensor_copy(
                        qv[:, et, c * CH:(c + 1) * CH], ps[iq][:]),
                        waits=[w(sPE)], incs=[(sDV, 1)])
                else:
                    def qrope(e, et=et, c=c, iq=iq):
                        cs = slice(c * CH, (c + 1) * CH)
                        e.tensor_mul(tq[0][:], ps[iq][:], tabs_c[:, cs])
                        e.tensor_mul(ps[5][:], ps[iq][:], tabs_s[:, cs])
                        e.tensor_sub(qv[0:64, et, cs], tq[0][0:64, :], ps[5][64:128, :])
                        return e.tensor_add(qv[64:128, et, cs], tq[0][64:128, :], ps[5][0:64, :])
                    rec = P.op("vector", qrope, waits=[w(sPE), (sDV, tab_tick)], incs=[(sDV, 1)])
                    pg[5] = (sDV, rec[0])
                pg[iq] = (sDV, rec[0])
        kpart = cc_in[0:KPART].rearrange("(e p t) -> e p t", p=128, t=T)
        for et in range(NG):
            sb, t1, slB = rB.load(wk_src.rearrange("(n p) m -> p n m", p=128)[:, :, et * 128:(et + 1) * 128])
            for c in range(NCH):
                ik = 2 + (c % 2)

                def kmm(e, sb=sb, c=c, ik=ik):
                    for dt in range(ND):
                        i = e.matmul(ps[ik][:], sb[:, dt, :], xnv[:, dt, c * CH:(c + 1) * CH],
                                     start=(dt == 0), stop=(dt == ND - 1))
                    return i
                rec0 = P.op("tensor", kmm, waits=[(rB.s_ld, t1), (sDV, ln_done[c])] + bank_wait(ik),
                            incs=[(sPE, 1)])
                if c == NCH - 1:
                    rB.free[slB] = (sPE, rec0[0])
                ksl = (et * NCH + c) % 2

                def kevac(e, c=c, ik=ik, ksl=ksl, is_a=is_a):
                    cs = slice(c * CH, (c + 1) * CH)
                    if not is_a:
                        e.tensor_mul(tq[0][:], ps[ik][:], tabs_c[:, cs])
                        e.tensor_mul(ps[5][:], ps[ik][:], tabs_s[:, cs])
                        e.tensor_sub(kstage[ksl][0:64, :], tq[0][0:64, :], ps[5][64:128, :])
                        e.tensor_add(kstage[ksl][64:128, :], tq[0][64:128, :], ps[5][0:64, :])
                        e.tensor_scalar_mul(tq[0][:], kstage[ksl][:], S_HI)
                        e.tensor_scalar_mul(tq[1][:], kstage[ksl][:], S_LO)
                        return e.tensor_add(kstage[ksl][:], tq[0][:], tq[1][:])
                    e.tensor_scalar_mul(tq[0][:], ps[ik][:], S_HI)
                    e.tensor_scalar_mul(tq[1][:], ps[ik][:], S_LO)
                    return e.tensor_add(kstage[ksl][:], tq[0][:], tq[1][:])
                kw = [w(sPE), state["kst_free"][ksl]]
                if not is_a:
                    kw.append((sDV, tab_tick))
                rec = P.op("vector", kevac, waits=kw, incs=[(sDV, 1)])
                if not is_a:
                    pg[5] = (sDV, rec[0])
                pg[ik] = (sDV, rec[0])
                rec = P.op("sync", lambda e, et=et, c=c, ksl=ksl: e.dma_start(
                    out=kpart[et, :, c * CH:(c + 1) * CH], in_=kstage[ksl][:]),
                    waits=[w(sDV)], incs=[(sCI, 16)])
                state["kst_free"][ksl] = (sCI, rec[0])
        vpart = cc_in[KPART:KPART + VPART].rearrange("(n p m) -> n p m", p=128, m=512)
        for strip in range(4):
            sb, t1, slB = rB.load(wv_src.rearrange("(n p) m -> p n m", p=128)[:, :, strip * 128:(strip + 1) * 128])
            for c in range(NCH):
                for tt in range(4):
                    iv = 2 + ((c * 4 + tt) % 2)

                    def vmm(e, sb=sb, c=c, tt=tt, iv=iv):
                        tsl = slice(c * CH + tt * 128, c * CH + tt * 128 + 128)
                        for dt in range(ND):
                            i = e.matmul(ps[iv][:, 0:128], xnv[:, dt, tsl], sb[:, dt, :],
                                         start=(dt == 0), stop=(dt == ND - 1))
                        return i
                    rec0 = P.op("tensor", vmm, waits=[(rB.s_ld, t1), (sDV, ln_done[c])] + bank_wait(iv),
                                incs=[(sPE, 1)])
                    if c == NCH - 1 and tt == 3:
                        rB.free[slB] = (sPE, rec0[0])
                    rec = P.op("scalar", lambda e, iv=iv: e.activation(
                        vstage[:], ps[iv][:, 0:128], AF.Copy),
                        waits=[w(sPE), state["vst_free"]], incs=[(sAC, 1)])
                    pg[iv] = (sAC, rec[0])
                    rec = P.op("sync", lambda e, c=c, tt=tt, st=strip: e.dma_start(
                        out=vpart[4 * c + tt, :, st * 128:(st + 1) * 128], in_=vstage[:]),
                        waits=[w(sAC)], incs=[(sCI, 16)])
                    state["vst_free"] = (sCI, rec[0])
        P.op("gpsimd", lambda e: e.collective_compute(
            "AllGather", ALU.bypass,
            replica_groups=[[0, 1], [2, 3], [4, 5], [6, 7]],
            ins=[cc_in[:]], outs=[cc_out[:]]),
            waits=[w(sCI)], incs=[(sCC, 1)])
        up_w = [w(sCC), w(sPE)]
        for r in range(2):
            ksrc = cc_out[r, 0:KPART].rearrange("(e p t) -> e p t", p=128, t=T)
            for et in range(NG):
                P.op("gpsimd", lambda e, r=r, et=et, ks=ksrc: e.dma_start(
                    out=Kv[:, et, :].rearrange("p (g m) -> p g m", g=8)[:, r:8:2, :],
                    in_=ks[et].rearrange("p (c m) -> p c m", c=4)),
                    waits=up_w, incs=[(sKV, 16)])
            vsrc = cc_out[r, KPART:KPART + VPART].rearrange(
                "(c s p m) -> p c s m", c=4, s=4, p=128)
            for cch in range(NCH):
                gk = (2 * cch + r) * 4 * 512
                P.op("gpsimd", lambda e, vs=vsrc, cch=cch, gk=gk: e.dma_start(
                    out=reg1[:, 16384 + gk:16384 + gk + 2048].rearrange(
                        "p (s m) -> p s m", s=4),
                    in_=vs[:, cch, :, :]),
                    waits=up_w, incs=[(sKV, 16)])
        kv_tick = sKV.n
        hp = 0
        for c in range(NCH):
            for h in range(NH):
                g = h // 4
                nkt = (2 * c + 2) if is_a else 8
                nsub = nkt * 4
                ic = 4 + (hp % 2)
                hp += 1
                for ks in range(nsub):
                    kt512 = ks // 4
                    isb = 2 + (ks % 2)
                    psl = ks % 4
                    P.op("tensor", lambda e, g=g, ks=ks, isb=isb, h=h, c=c: e.matmul(
                        ps[isb][:], Kv[:, g, ks * 128:(ks + 1) * 128],
                        qv[:, h, c * CH:(c + 1) * CH], start=True, stop=True),
                        waits=[(sKV, kv_tick)] + bank_wait(isb), incs=[(sPE, 1)])
                    rec = P.op("scalar", lambda e, isb=isb, p=psl: e.activation(
                        p_sb[:, p, :], ps[isb][:], AF.Exp),
                        waits=[w(sPE)], incs=[(sAC, 1)])
                    pg[isb] = (sAC, rec[0])
                    pready = (sAC, rec[0])
                    if is_a and kt512 >= 2 * c:
                        j = kt512 - 2 * c
                        rec = P.op("vector", lambda e, p=psl, j=j, sx=ks % 4: e.tensor_mul(
                            p_sb[:, p, :], p_sb[:, p, :], bmask[:, j * 4 + sx, :]),
                            waits=[pready, (sMI, mask_tick)], incs=[(sDV, 1)])
                        pready = (sDV, rec[0])

                    def rspv(e, g=g, p=psl, ic=ic, first=(ks == 0), last=(ks == nsub - 1),
                             kt=ks):
                        e.matmul(ps[6][0:1, :], ones_bf[:], p_sb[:, p, :],
                                 start=first, stop=last)
                        return e.matmul(ps[ic][:], Vv[:, kt, g * 128:(g + 1) * 128],
                                        p_sb[:, p, :], start=first, stop=last)
                    waits = [pready]
                    if ks == 0:
                        waits += bank_wait(6) + bank_wait(ic)
                    P.op("tensor", rspv, waits=waits, incs=[(sPE, 1)])
                rec = P.op("vector", lambda e: e.reciprocal(rr_sb[:], ps[6][0:1, :]),
                           waits=[w(sPE)], incs=[(sDV, 1)])
                pg[6] = (sDV, rec[0])
                P.op("tensor", lambda e: e.matmul(ps[7][:], ones_f1[:], rr_sb[:],
                                                  start=True, stop=True),
                     waits=[w(sDV)] + bank_wait(7), incs=[(sPE, 1)])
                rec = P.op("scalar", lambda e: e.activation(bc_sb[:], ps[7][:], AF.Copy),
                           waits=[w(sPE)], incs=[(sAC, 1)])
                pg[7] = (sAC, rec[0])
                rec = P.op("vector", lambda e, h=h, ic=ic: e.tensor_mul(
                    ctx_sb[:, h, :], ps[ic][:], bc_sb[:]),
                    waits=[w(sAC)], incs=[(sDV, 1)])
                pg[ic] = (sDV, rec[0])
            ctx_ready = sDV.n
            for dt in range(ND):
                so, t1, slA = rA.load(wo_src.rearrange("(n p) m -> p n m", p=128)[:, :, dt * 128:(dt + 1) * 128])
                io = 0 + (dt % 2)

                def omm(e, so=so, io=io):
                    for et in range(NH):
                        i = e.matmul(ps[io][:], so[:, et, :], ctx_sb[:, et, :],
                                     start=(et == 0), stop=(et == NH - 1))
                    return i
                rec0 = P.op("tensor", omm, waits=[(rA.s_ld, t1), (sDV, ctx_ready)] + bank_wait(io),
                            incs=[(sPE, 1)])
                rA.free[slA] = (sPE, rec0[0])
                asl = (dt + c) % 2
                lw = [state["xacc_free"][asl]]
                ep = epoch.get((dt, c), 0)
                if ep:
                    lw.append((sXS, ep))
                P.op("sync", lambda e, s=xsrc, dt=dt, c=c, a=asl: e.dma_start(
                    out=xacc[a][:], in_=s[dt, :, c * CH:(c + 1) * CH]),
                    waits=lw, incs=[(sXL, 16)])
                rec = P.op("vector", lambda e, a=asl, io=io: e.tensor_add(
                    xacc[a][:], ps[io][:], xacc[a][:]),
                    waits=[w(sPE), w(sXL)], incs=[(sDV, 1)])
                pg[io] = (sDV, rec[0])
                state["xacc_free"][asl] = (sDV, rec[0])
                P.op("sync", lambda e, dt=dt, c=c, a=asl: e.dma_start(
                    out=x_hbm[dt, :, c * CH:(c + 1) * CH], in_=xacc[a][:]),
                    waits=[w(sDV)], incs=[(sXS, 16)])
                epoch[(dt, c)] = sXS.n

    sublayers = [("attn", "a"), ("ffn", None), ("attn", "b"), ("ffn", None)][:n_sub]
    xsrc0 = din["xT"].rearrange("(n p) t -> n p t", p=128)
    for sub, (skind, var) in enumerate(sublayers):
        xsrc = xsrc0 if sub == 0 else x_hbm
        is_last = sub == len(sublayers) - 1
        ln_done = emit_ln(sub, xsrc)
        if skind == "ffn":
            emit_ffn(sub, ln_done, xsrc, is_last)
        else:
            emit_attn(sub, var, ln_done, xsrc)
        bar = [w(sPE), w(sDV), w(sAC), w(sXS), w(sXL), w(sCI), w(sKV)]
        for eng in Prog.ENGINES:
            P.op(eng, lambda e: None, waits=list(bar))
        for i in range(8):
            pg[i] = None
        P.lastw = {e: dict(P.lastw[e]) for e in P.lastw}
    if sublayers[-1][0] == "attn":
        xov = xout.rearrange("(n p) t -> n p t", p=128)
        for dt in range(ND):
            P.op("sync", lambda e, dt=dt: e.dma_start(out=xov[dt], in_=x_hbm[dt]),
                 waits=[w(sXS)], incs=[(sOUT, 16)])
    P.op("sync", lambda e: None, waits=[w(sOUT)])

    with nc.Block() as block:
        P.finish(block)
    return nc



def _prep_inputs(core, inputs):
    role = core & 1
    b = core >> 1
    f = np.float32
    x = np.asarray(inputs["L0_attn_a_x"], dtype=f)
    xb = np.ascontiguousarray(x[b].reshape(8, CH, D)[role::2].reshape(T, D).T)
    masks = np.zeros((8, 128, CH), dtype=f)
    krow = np.arange(128)[:, None]
    qcol = np.arange(CH)[None, :]
    for j in range(2):
        for s2 in range(4):
            masks[j * 4 + s2] = (qcol + 512 * (role - j) - 128 * s2 - krow >= 0)
    pos = (np.arange(T) // CH * 2 + role) * CH + (np.arange(T) % CH)
    inv = 1.0 / (10000.0 ** (np.arange(0, HD, 2, dtype=np.float64) / HD))
    ang = np.concatenate([inv, inv])[:, None] * pos[None, :].astype(np.float64)
    cos64, sin64 = np.cos(ang), np.sin(ang)
    rc_hi = cos64.astype(f)
    rc_lo = (cos64 - rc_hi.astype(np.float64)).astype(f)
    rs_hi = sin64.astype(f)
    rs_lo = (sin64 - rs_hi.astype(np.float64)).astype(f)
    ca = np.ascontiguousarray
    return {
        "xT": xb,
        "aw_q": ca(inputs["L0_attn_a_w_q"], dtype=f),
        "aw_kv": ca(inputs["L0_attn_a_w_kv"], dtype=f),
        "aw_d": ca(inputs["L0_attn_a_w_dense"], dtype=f),
        "bw_q": ca(np.asarray(inputs["L0_attn_b_Wq"]).T, dtype=f),
        "bw_k": ca(np.asarray(inputs["L0_attn_b_Wk"]).T, dtype=f),
        "bw_v": ca(np.asarray(inputs["L0_attn_b_Wv"]).T, dtype=f),
        "bw_o": ca(np.asarray(inputs["L0_attn_b_Wo"]).T, dtype=f),
        "f_w1": ca(inputs["L0_ffn_W1"], dtype=f),
        "f_wg": ca(inputs["L0_ffn_Wg"], dtype=f),
        "f_w2": ca(inputs["L0_ffn_W2"], dtype=f),
        "f_s1": ca(inputs["L0_ffn_Ws1"], dtype=f),
        "f_sg": ca(inputs["L0_ffn_Wsg"], dtype=f),
        "f_s2": ca(inputs["L0_ffn_Ws2"], dtype=f),
        "f_gw": ca(inputs["L0_ffn_gate_w"], dtype=f),
        "f_gb": ca(np.asarray(inputs["L0_ffn_gate_b"], dtype=f).reshape(NREXP, 1)),
        "f_bias": ca(np.broadcast_to(np.asarray(inputs["L0_ffn_biases"], dtype=f)[None, :],
                                     (128, NREXP))),
        "masks": masks,
        "rc_hi": rc_hi, "rc_lo": rc_lo, "rs_hi": rs_hi, "rs_lo": rs_lo,
        "ident": np.eye(128, dtype=f),
        "sel7": ca(np.kron(np.eye(NREXP, dtype=f), np.ones((1, 128), dtype=f))),
    }


def kernel(**inputs):
    nc = build_nc(n_sub=4)
    in_maps = [_prep_inputs(core, inputs) for core in range(8)]
    res = run_bass_kernel_spmd(nc, in_maps, core_ids=list(range(8)))
    out = np.zeros((4, S, D), dtype=np.float32)
    for core in range(8):
        role, b = core & 1, core >> 1
        shard = res.results[core]["xout"].T.reshape(NCH, CH, D)
        ob = out[b].reshape(8, CH, D)
        ob[role::2] = shard
    return out
